# version 02
# baseline (speedup 1.0000x reference)
import numpy as np
import ml_dtypes
from contextlib import ExitStack
import concourse.bass as bass
import concourse.mybir as mybir

F32 = mybir.dt.float32
BF16 = mybir.dt.bfloat16
AF = mybir.ActivationFunctionType
ALU = mybir.AluOpType
AX = mybir.AxisListType
BF = ml_dtypes.bfloat16

NT = 2048
HALO = 640
NTH = NT + HALO
DM = 1024

ENGS = ("TE", "ACT", "DVE", "POOL", "SYNC")
DMAQ = {"SYNC": "QSP", "ACT": "QACT", "POOL": "QPOOL"}


class Sched:
    def __init__(self):
        self.items = {e: [] for e in ENGS}
        self.c = {"TE": 0, "ACT": 0, "DVE": 0, "POOL": 0,
                  "QSP": 0, "QACT": 0, "QPOOL": 0}
        self.lw = {}
        self.rd = {}

    def _tok_of(self, eng, q, n):
        q = q or eng
        self.c[q] += n
        return q, self.c[q]

    def op(self, eng, fn, reads=(), writes=(), waits=(), q=None, n=1):
        wl = {}

        def addw(tok):
            if tok is None:
                return
            qq, vv = tok
            if vv > 0 and wl.get(qq, 0) < vv:
                wl[qq] = vv

        for t in waits:
            addw(t)
        for k in reads:
            addw(self.lw.get(k))
        for k in writes:
            addw(self.lw.get(k))
            for qq, vv in self.rd.get(k, {}).items():
                addw((qq, vv))
        q = q or eng
        if q.startswith("Q"):
            addw((q, self.c[q]))
        self.c[q] += n
        tok = (q, self.c[q])
        self.items[eng].append((fn, sorted(wl.items()), q, n))
        for k in reads:
            d = self.rd.setdefault(k, {})
            if d.get(tok[0], 0) < tok[1]:
                d[tok[0]] = tok[1]
        for k in writes:
            self.lw[k] = tok
            self.rd[k] = {}
        return tok

    def dma(self, eng, out, in_, reads=(), writes=(), waits=()):
        return self.op(eng, lambda e, o=out, i=in_: e.dma_start(out=o, in_=i),
                       reads, writes, waits, q=DMAQ[eng], n=16)

    def mmg(self, bank, fns, reads=(), waits=()):
        tok = self.op("TE", fns[0], reads=reads, writes=[bank], waits=waits)
        for fn in fns[1:]:
            tok = self.op("TE", fn)
        self.lw[bank] = tok
        for k in reads:
            d = self.rd.setdefault(k, {})
            if d.get(tok[0], 0) < tok[1]:
                d[tok[0]] = tok[1]
        return tok

    def now(self, *qs):
        return [(q, self.c[q]) for q in qs]

    def barrier_tokens(self):
        return self.now("TE", "ACT", "DVE", "POOL", "QSP", "QACT", "QPOOL")

    def barrier(self):
        toks = [t for t in self.barrier_tokens() if t[1] > 0]
        for e in ENGS:
            self.items[e].append((None, toks, None, 0))

    def replay(self, blk, sems):
        def mk(eng_items):
            def body(eng):
                for fn, waits, q, n in eng_items:
                    for qq, vv in waits:
                        eng.wait_ge(sems[qq], vv)
                    if fn is not None:
                        fn(eng).then_inc(sems[q], n)
            return body
        blk.tensor(mk(self.items["TE"]))
        blk.scalar(mk(self.items["ACT"]))
        blk.vector(mk(self.items["DVE"]))
        blk.gpsimd(mk(self.items["POOL"]))
        blk.sync(mk(self.items["SYNC"]))


def make_sems(nc, ctx):
    names = ("TE", "ACT", "DVE", "POOL", "QSP", "QACT", "QPOOL")
    return {q: ctx.enter_context(nc.semaphore("s_" + q.lower())) for q in names}


def _bf(x):
    return np.ascontiguousarray(np.asarray(x), dtype=BF)


def _f32(x):
    return np.ascontiguousarray(np.asarray(x), dtype=np.float32)


def _kc(x, p=128):
    K, N = x.shape
    return np.ascontiguousarray(x.reshape(K // p, p, N).transpose(1, 0, 2).reshape(p, -1))


def _ctile_cols(v, p=128):
    return np.ascontiguousarray(np.asarray(v, dtype=np.float32).reshape(-1, p).T)


def rotary_tables_half(t0, n):
    D = 64
    inv_freq = 1.0 / 10000.0 ** (np.arange(0, D, 2, dtype=np.float32) / D)
    pos = np.arange(t0, t0 + n, dtype=np.float32)
    fr = inv_freq[np.arange(D) % 32][:, None] * pos[None, :]
    return (np.concatenate([np.cos(fr)] * 2, 0).astype(np.float32),
            np.concatenate([np.sin(fr)] * 2, 0).astype(np.float32))


def rot_lhsT():
    R = np.zeros((128, 128), dtype=np.float32)
    for base in (0, 64):
        for d in range(32):
            R[base + d, base + d + 32] = -1.0
            R[base + d + 32, base + d] = 1.0
    return np.ascontiguousarray(R.T)


def l0_masks(half):
    out = np.zeros((128, 2 * 8 * 512), dtype=np.float32)
    NEG = -30000.0
    r = np.arange(128)[:, None]
    j = np.arange(512)[None, :]
    for m in range(8):
        d = 512 - 128 * m + j - r
        std = np.where((d >= 0) & (d <= 511), 0.0, NEG)
        out[:, m * 512:(m + 1) * 512] = std
        dead = np.full((128, 512), NEG, dtype=np.float32)
        first = dead if (half == 0 and m < 4) else std
        out[:, (8 + m) * 512:(9 + m) * 512] = first
    return _bf(out)


def hostA(inp, core):
    b, h = core // 2, core % 2
    x = _f32(inp["L0_attn_x"])
    own = x[b, h * 2048:(h + 1) * 2048]
    halo = (np.zeros((HALO, DM), np.float32) if h == 0
            else x[b, 2048 - HALO:2048])
    xh = np.concatenate([halo, own], 0)
    cos, sin = rotary_tables_half(h * 2048 - HALO, NTH)
    if h == 0:
        c2_, s2_ = rotary_tables_half(0, NTH)
        cos[:, :HALO] = c2_[:, :HALO]
        sin[:, :HALO] = s2_[:, :HALO]
    cdiag = {}
    for p in "qkv":
        w = _f32(inp[f"L0_attn_c{p}w"])
        d = np.zeros((128, 8 * 4 * 128), np.float32)
        for cch in range(8):
            for j in range(4):
                blk = np.zeros((128, 128), np.float32)
                np.fill_diagonal(blk, w[cch * 128:(cch + 1) * 128, j])
                d[:, (cch * 4 + j) * 128:(cch * 4 + j + 1) * 128] = blk
        cdiag[p] = d
    grp2 = np.zeros((128, 2), np.float32)
    grp2[:64, 0] = 1.0
    grp2[64:, 1] = 1.0
    return {
        "xh": _f32(xh),
        "wqT": _kc(_f32(inp["L0_attn_Wq"]).T),
        "wkT": _kc(_f32(inp["L0_attn_Wk"]).T),
        "wvT": _kc(_f32(inp["L0_attn_Wv"]).T),
        "woT": _kc(_f32(inp["L0_attn_Wo"]).T),
        "bq": _ctile_cols(inp["L0_attn_bq"]),
        "bk": _ctile_cols(inp["L0_attn_bk"]),
        "bv": _ctile_cols(inp["L0_attn_bv"]),
        "cbq": _ctile_cols(inp["L0_attn_cqb"]),
        "cbk": _ctile_cols(inp["L0_attn_ckb"]),
        "cbv": _ctile_cols(inp["L0_attn_cvb"]),
        "cdq": cdiag["q"], "cdk": cdiag["k"], "cdv": cdiag["v"],
        "cosT": cos, "sinT": sin, "rotT": rot_lhsT(),
        "grp2": grp2, "bc2": np.ascontiguousarray(grp2.T),
        "ones64": np.ones((128, 64), np.float32),
        "ident": np.eye(128, dtype=np.float32),
        "masks": l0_masks(h),
        "bo_bc": np.broadcast_to(_f32(inp["L0_attn_bo"]), (128, 1024)).copy(),
        "wr": _kc(_f32(inp["L0_ffn_Wr"])),
        "br_bc": np.broadcast_to(_f32(inp["L0_ffn_br"]), (128, 8)).copy(),
        "consts": np.tile(np.array(
            [0.125, 1e-12, 1e-5, 1.0 / 1024, 0.0, 1.0, -1.0, 0.5], np.float32),
            (128, 1)),
    }


def emit_ln(S, src_ap, dst_ap, mst, sq_scratch, c3, c2, rk_src, wk_dst,
            rk_extra=()):
    mk, sk = "mstk", "sqk"
    rk_src = list(rk_src) if isinstance(rk_src, list) else [rk_src]
    S.op("DVE", lambda e: e.reduce_sum(mst[:, 0:1], src_ap, axis=AX.X),
         reads=rk_src, writes=[mk])
    S.op("ACT", lambda e: e.activation(sq_scratch, src_ap, AF.Square),
         reads=rk_src, writes=[sk])
    S.op("DVE", lambda e: e.reduce_sum(mst[:, 1:2], sq_scratch, axis=AX.X),
         reads=[sk], writes=[mk])
    S.op("DVE", lambda e: e.tensor_scalar(mst[:, 2:3], mst[:, 0:1], c3, None, ALU.mult),
         reads=[mk], writes=[mk])
    S.op("DVE", lambda e: e.tensor_scalar(mst[:, 3:4], mst[:, 1:2], c3, None, ALU.mult),
         reads=[mk], writes=[mk])
    S.op("DVE", lambda e: e.tensor_mul(mst[:, 4:5], mst[:, 2:3], mst[:, 2:3]),
         reads=[mk], writes=[mk])
    S.op("DVE", lambda e: e.tensor_sub(mst[:, 5:6], mst[:, 3:4], mst[:, 4:5]),
         reads=[mk], writes=[mk])
    S.op("ACT", lambda e: e.activation(mst[:, 6:7], mst[:, 5:6], AF.Sqrt, bias=c2),
         reads=[mk], writes=[mk])
    S.op("DVE", lambda e: e.reciprocal(mst[:, 6:7], mst[:, 6:7]),
         reads=[mk], writes=[mk])
    S.op("DVE", lambda e: e.tensor_mul(mst[:, 7:8], mst[:, 2:3], mst[:, 6:7]),
         reads=[mk], writes=[mk])
    S.op("DVE", lambda e: e.tensor_scalar(mst[:, 7:8], mst[:, 7:8], -1.0, None, ALU.mult),
         reads=[mk], writes=[mk])
    return S.op("ACT", lambda e: e.activation(dst_ap, src_ap, AF.Identity,
                                              bias=mst[:, 7:8], scale=mst[:, 6:7]),
                reads=[mk] + rk_src + list(rk_extra), writes=[wk_dst])


CH = [512] * 5 + [128]
CH0 = [0, 512, 1024, 1536, 2048, 2560]


def build_A():
    nc = bass.Bass("TRN2", target_bir_lowering=False)
    g = lambda n, s, dt=F32: nc.dram_tensor(n, s, dt, kind="ExternalInput").ap()

    xh = g("xh", [NTH, DM])
    wT = {p: g(f"w{p}T", [128, 8 * 1024]) for p in "qkv"}
    woT = g("woT", [128, 8 * 1024])
    bias_in = {p: g(f"b{p}", [128, 8]) for p in "qkv"}
    cbias_in = {p: g(f"cb{p}", [128, 8]) for p in "qkv"}
    cdiag_in = {p: g(f"cd{p}", [128, 8 * 4 * 128]) for p in "qkv"}
    cosT_in = g("cosT", [128, NTH])
    sinT_in = g("sinT", [128, NTH])
    rotT_in = g("rotT", [128, 128])
    grp2_in = g("grp2", [128, 2])
    bc2_in = g("bc2", [2, 128])
    ones64_in = g("ones64", [128, 64])
    ident_in = g("ident", [128, 128])
    masks_in = g("masks", [128, 8192], BF16)
    bo_in = g("bo_bc", [128, 1024])
    wr_in = g("wr", [128, 64])
    br_in = g("br_bc", [128, 8])
    consts_in = g("consts", [128, 8])

    x1_out = nc.dram_tensor("x1", [NT, DM], F32, kind="ExternalOutput").ap()
    probs_out = nc.dram_tensor("probs", [NT, 8], F32, kind="ExternalOutput").ap()

    qhbm = nc.dram_tensor("qhbm", [128, 8 * NTH], BF16, kind="Internal").ap()
    khbm = nc.dram_tensor("khbm", [128, 8 * NTH], BF16, kind="Internal").ap()
    vhbm = nc.dram_tensor("vhbm", [128, 21 * 1024], BF16, kind="Internal").ap()

    with ExitStack() as ctx:
        sb = lambda n, s, dt: ctx.enter_context(nc.sbuf_tensor(n, s, dt))
        psm = lambda n, s, dt=F32: ctx.enter_context(nc.psum_tensor(n, s, dt))

        bias_sb = {p: sb(f"b{p}sb", [128, 8], F32) for p in "qkv"}
        cbias_sb = {p: sb(f"cb{p}sb", [128, 8], F32) for p in "qkv"}
        bo_sb = sb("bosb", [128, 1024], F32)
        br_sb = sb("brsb", [128, 8], F32)
        cst = sb("cst", [128, 8], F32)
        id_bf = sb("idbf", [128, 128], BF16)
        rot_bf = sb("rotbf", [128, 128], BF16)
        grp_bf = sb("grpbf", [128, 2], BF16)
        bc2_bf = sb("bc2bf", [2, 128], BF16)
        on64_bf = sb("on64bf", [128, 64], BF16)
        wr_bf = sb("wrbf", [128, 64], BF16)
        mst = sb("mst", [128, 8], F32)

        pb = [psm(f"pb{i}", [128, 512]) for i in range(6)]
        pt_bf = [psm(f"ptb{i}", [128, 512], BF16) for i in range(2)]
        PB = [f"pb{i}" for i in range(6)]
        PT = ["pt0", "pt1"]

        S = Sched()
        sems = make_sems(nc, ctx)

        c1 = cst[:, 1:2]
        c2 = cst[:, 2:3]
        c3 = cst[:, 3:4]

        with ExitStack() as p1:
            sb1 = lambda n, s, dt: p1.enter_context(nc.sbuf_tensor(n, s, dt))
            lnxT = sb1("lnxT", [128, 8 * NTH], BF16)
            stage = sb1("stage", [128, 2048], F32)
            w_bf = sb1("w_bf", [128, 8 * 1024], BF16)
            cd_bf = sb1("cd_bf", [128, 4 * 128 * 8], BF16)
            cos_bf = sb1("cosbf", [128, NTH], BF16)
            sin_bf = sb1("sinbf", [128, NTH], BF16)
            xt = sb1("xt", [128, 1024], F32)
            sqs = sb1("sqs", [128, 1024], F32)
            lnx = sb1("lnx", [128, 1024], BF16)
            plin = {b: sb1(f"plin{b}", [128, 8 * 512], BF16) for b in range(2)}
            pc = sb1("pc", [128, 8 * 512], BF16)
            qch = sb1("qch", [128, 8 * 512], BF16)
            vch = sb1("vch", [128, 4 * 1024], BF16)
            tmp1 = sb1("tmp1", [128, 512], F32)
            tmp2 = sb1("tmp2", [128, 512], F32)
            tmp3 = sb1("tmp3", [128, 512], BF16)
            sqb = sb1("sqb", [128, 512], BF16)
            rs_f = sb1("rs_f", [2, 512], F32)
            rs_b = sb1("rs_b", [2, 512], BF16)
            rb_bf = sb1("rb_bf", [128, 512], BF16)

            for p in "qkv":
                S.dma("SYNC", bias_sb[p][:], bias_in[p], writes=[f"b{p}"])
                S.dma("SYNC", cbias_sb[p][:], cbias_in[p], writes=[f"cb{p}"])
            S.dma("SYNC", bo_sb[:], bo_in, writes=["bo"])
            S.dma("SYNC", br_sb[:], br_in, writes=["br"])
            S.dma("SYNC", cst[:], consts_in, writes=["cst"])

            def conv_in(src_ap, dst_ap, width, key):
                for off in range(0, width, 2048):
                    wdt = min(2048, width - off)
                    sl = stage[:, 0:wdt]
                    S.dma("SYNC", sl, src_ap[:, off:off + wdt], writes=["stage"])
                    S.op("DVE", lambda e, d=dst_ap[:, off:off + wdt], s=sl:
                         e.tensor_copy(d, s), reads=["stage"], writes=[key])

            conv_in(cosT_in, cos_bf[:], NTH, "cos")
            conv_in(sinT_in, sin_bf[:], NTH, "sin")
            conv_in(rotT_in, rot_bf[:], 128, "rot")
            conv_in(ident_in, id_bf[:], 128, "id")
            conv_in(ones64_in, on64_bf[:], 64, "on64")
            conv_in(wr_in, wr_bf[:], 64, "wrk")
            S.dma("SYNC", stage[0:128, 0:2], grp2_in, writes=["stage"])
            S.op("DVE", lambda e: e.tensor_copy(grp_bf[:], stage[0:128, 0:2]),
                 reads=["stage"], writes=["grp"])
            S.dma("SYNC", stage[0:2, 0:128], bc2_in, writes=["stage"])
            S.op("DVE", lambda e: e.tensor_copy(bc2_bf[:], stage[0:2, 0:128]),
                 reads=["stage"], writes=["bc2"])

            for tt in range(21):
                S.dma("SYNC", xt[:], xh[tt * 128:(tt + 1) * 128, :], writes=["xt"])
                emit_ln(S, xt[:], lnx[:], mst, sqs[:], c3, c2, "xt", "lnx",
                        rk_extra=["cst"])
                for half in range(2):
                    pti = half
                    fns = []
                    for q4 in range(4):
                        dch = half * 4 + q4
                        fns.append(lambda e, o=pt_bf[pti][:, q4 * 128:(q4 + 1) * 128],
                                   i=lnx[:, dch * 128:(dch + 1) * 128]:
                                   e.transpose(o, i, id_bf[:]))
                    tok = S.op("TE", fns[0], reads=["lnx", "id"], writes=[PT[pti]])
                    for f in fns[1:]:
                        tok = S.op("TE", f)
                    S.lw[PT[pti]] = tok

                    def ev(e, pt=pt_bf[pti], half=half, tt=tt):
                        src = pt[:, 0:512].rearrange("p (a b) -> p a b", a=4)
                        d = lnxT[:].rearrange("p (a b) -> p a b", a=8, b=NTH)[
                            :, half * 4: half * 4 + 4, tt * 128: tt * 128 + 128]
                        return e.tensor_copy(d, src)
                    S.op("DVE", ev, reads=[PT[pti]], writes=[("lnxT", tt)])
            lnxT_keys = [("lnxT", tt) for tt in range(21)]

            pbi = [0]

            def next_pb():
                i = pbi[0]
                pbi[0] = (i + 1) % 4
                return i

            for p in "qkv":
                conv_in(wT[p], w_bf[:], 8192, "w_bf")
                conv_in(cdiag_in[p], cd_bf[:], 4096, "cd_bf")
                for ci, (n, t0) in enumerate(zip(CH, CH0)):
                    ntile = n // 128
                    buf = ci % 2
                    for cch in range(8):
                        bi = next_pb()
                        fns = [lambda e, o=pb[bi][:, 0:n],
                               l=w_bf[:, kch * 1024 + cch * 128: kch * 1024 + (cch + 1) * 128],
                               r=lnxT[:, kch * NTH + t0: kch * NTH + t0 + n],
                               st=(kch == 0), sp=(kch == 7):
                               e.matmul(o, l, r, start=st, stop=sp) for kch in range(8)]
                        S.mmg(PB[bi], fns, reads=lnxT_keys + ["w_bf"])
                        S.op("ACT", lambda e, o=plin[buf][:, cch * 512: cch * 512 + n],
                             i=pb[bi][:, 0:n], b=bias_sb[p][:, cch:cch + 1]:
                             e.activation(o, i, AF.Identity, bias=b),
                             reads=[PB[bi], f"b{p}"], writes=[("plin", buf, cch)])
                    prev_buf = 1 - buf
                    prevlen = CH[ci - 1] if ci > 0 else 0
                    for cch in range(8):
                        bi = next_pb()
                        specs = []
                        for j in [3, 2, 1, 0]:
                            sh = 3 - j
                            l_ap = cd_bf[:, (cch * 4 + j) * 128:(cch * 4 + j + 1) * 128]
                            if n - sh > 0:
                                specs.append((pb[bi][:, sh:n], l_ap,
                                              plin[buf][:, cch * 512: cch * 512 + n - sh]))
                            if sh > 0 and ci > 0:
                                specs.append((pb[bi][:, 0:sh], l_ap,
                                              plin[prev_buf][:, cch * 512 + prevlen - sh:
                                                             cch * 512 + prevlen]))
                        fns = [lambda e, o=o_, l=l_, r=r_, st=(i == 0),
                               sp=(i == len(specs) - 1):
                               e.matmul(o, l, r, start=st, stop=sp)
                               for i, (o_, l_, r_) in enumerate(specs)]
                        rd = [("plin", buf, cch), "cd_bf"]
                        if ci > 0:
                            rd.append(("plin", prev_buf, cch))
                        S.mmg(PB[bi], fns, reads=rd)
                        S.op("ACT", lambda e, o=pc[:, cch * 512: cch * 512 + n],
                             i=pb[bi][:, 0:n], b=cbias_sb[p][:, cch:cch + 1]:
                             e.activation(o, i, AF.Identity, bias=b),
                             reads=[PB[bi], f"cb{p}"], writes=[("pc", cch)])
                    if p in "qk":
                        for cch in range(8):
                            pcsl = pc[:, cch * 512: cch * 512 + n]
                            pck = ("pc", cch)
                            bi = next_pb()
                            S.mmg(PB[bi], [lambda e, o=pb[bi][:, 0:n], r=pcsl:
                                           e.matmul(o, rot_bf[:], r, start=True, stop=True)],
                                  reads=[pck, "rot"])
                            S.op("DVE", lambda e, o=tmp1[:, 0:n], a=pcsl,
                                 b=cos_bf[:, t0:t0 + n]: e.tensor_mul(o, a, b),
                                 reads=[pck, "cos"], writes=["tmp1"])
                            S.op("ACT", lambda e, o=tmp3[:, 0:n], i=pb[bi][:, 0:n]:
                                 e.activation(o, i, AF.Copy), reads=[PB[bi]], writes=["tmp3"])
                            S.op("DVE", lambda e, o=tmp2[:, 0:n], a=tmp3[:, 0:n],
                                 b=sin_bf[:, t0:t0 + n]: e.tensor_mul(o, a, b),
                                 reads=["tmp3", "sin"], writes=["tmp2"])
                            S.op("DVE", lambda e, o=pcsl, a=tmp1[:, 0:n], b=tmp2[:, 0:n]:
                                 e.tensor_add(o, a, b), reads=["tmp1", "tmp2"], writes=[pck])
                            S.op("DVE", lambda e, o=sqb[:, 0:n], a=pcsl, b=pcsl:
                                 e.tensor_mul(o, a, b), reads=[pck], writes=["sqb"])
                            bi2 = next_pb()
                            S.mmg(PB[bi2], [lambda e, o=pb[bi2][0:2, 0:n], r=sqb[:, 0:n]:
                                            e.matmul(o, grp_bf[:], r, start=True, stop=True)],
                                  reads=["sqb", "grp"])
                            S.op("ACT", lambda e, o=rs_f[:, 0:n], i=pb[bi2][0:2, 0:n]:
                                 e.activation(o, i, AF.Sqrt, bias=c1[0:2, :]),
                                 reads=[PB[bi2], "cst"], writes=["rs_f"])
                            S.op("DVE", lambda e, o=rs_f[:, 0:n]: e.reciprocal(o, o),
                                 reads=["rs_f"], writes=["rs_f"])
                            S.op("DVE", lambda e, o=rs_b[:, 0:n], i=rs_f[:, 0:n]:
                                 e.tensor_copy(o, i),
                                 reads=["rs_f"], writes=["rs_b"])
                            bi3 = next_pb()
                            S.mmg(PB[bi3], [lambda e, o=pb[bi3][:, 0:n], r=rs_b[0:2, 0:n]:
                                            e.matmul(o, bc2_bf[:], r, start=True, stop=True)],
                                  reads=["rs_b", "bc2"])
                            S.op("ACT", lambda e, o=rb_bf[:, 0:n], i=pb[bi3][:, 0:n]:
                                 e.activation(o, i, AF.Copy), reads=[PB[bi3]], writes=["rb_bf"])
                            S.op("DVE", lambda e, o=qch[:, cch * 512: cch * 512 + n],
                                 a=pcsl, b=rb_bf[:, 0:n]: e.tensor_mul(o, a, b),
                                 reads=[pck, "rb_bf"], writes=[("qch", cch)])
                        dst_hbm = qhbm if p == "q" else khbm
                        for cch in range(8):
                            S.dma("SYNC", dst_hbm[:, cch * NTH + t0: cch * NTH + t0 + n],
                                  qch[:, cch * 512: cch * 512 + n],
                                  reads=[("qch", cch)],
                                  writes=[(p + "hbm", ci, cch)])
                    else:
                        for cch in range(8):
                            for k in range(ntile):
                                pti = k % 2
                                S.op("TE", lambda e, o=pt_bf[pti][:, 0:128],
                                     i=pc[:, cch * 512 + k * 128: cch * 512 + (k + 1) * 128]:
                                     e.transpose(o, i, id_bf[:]),
                                     reads=[("pc", cch), "id"], writes=[PT[pti]])
                                S.op("DVE", lambda e, o=vch[:, k * 1024 + cch * 128:
                                                           k * 1024 + (cch + 1) * 128],
                                     i=pt_bf[pti][:, 0:128]: e.tensor_copy(o, i),
                                     reads=[PT[pti]], writes=[("vch", k, cch)])
                        S.dma("SYNC", vhbm[:, (t0 // 128) * 1024: (t0 // 128 + ntile) * 1024],
                              vch[:, 0: ntile * 1024],
                              reads=[("vch", k_, c_) for k_ in range(ntile) for c_ in range(8)],
                              writes=[("vhbm", ci)])

            S.barrier()

        with ExitStack() as p2:
            sb2 = lambda n, s, dt: p2.enter_context(nc.sbuf_tensor(n, s, dt))
            mask_sb = sb2("masksb", [128, 8192], BF16)
            wo_bf = sb2("wobf", [128, 8 * 1024], BF16)
            stage2 = sb2("stage2", [128, 2048], F32)
            qT_sb = sb2("qTsb", [128, 8 * 512], BF16)
            kT_sb = sb2("kTsb", [128, 8 * 1024], BF16)
            vt_sb = sb2("vtsb", [128, 8 * 1024], BF16)
            outT_ch = sb2("outTch", [128, 8 * 512], BF16)
            pT = [sb2(f"pT{i}", [128, 512], BF16) for i in range(2)]
            dnr = sb2("dnr", [128, 512], F32)
            y_sb = sb2("y_sb", [128, 1024], F32)
            lt_sb = sb2("lt_sb", [128, 8 * 128], BF16)
            lg_sb = sb2("lg_sb", [128, 8], F32)
            m_sb = sb2("m_sb", [128, 2], F32)
            xt2 = sb2("xt2", [128, 1024], F32)
            lnx2 = sb2("lnx2", [128, 1024], BF16)
            sqs2 = sb2("sqs2", [128, 1024], F32)

            S.dma("SYNC", mask_sb[:], masks_in, writes=["mask"])
            for off in range(0, 8192, 2048):
                sl = stage2[:, 0:2048]
                S.dma("SYNC", sl, woT[:, off:off + 2048], writes=["stage2"])
                S.op("DVE", lambda e, d=wo_bf[:, off:off + 2048], s=sl:
                     e.tensor_copy(d, s), reads=["stage2"], writes=["wo"])

            spill_keys = ([("qhbm", ci, cc) for ci in range(6) for cc in range(8)] +
                          [("khbm", ci, cc) for ci in range(6) for cc in range(8)] +
                          [("vhbm", ci) for ci in range(6)])
            for qc in range(4):
                qoff = HALO + 512 * qc
                koff0 = qoff - 512
                mset = 1 if qc == 0 else 0
                for cch in range(8):
                    S.dma("SYNC", qT_sb[:, cch * 512:(cch + 1) * 512],
                          qhbm[:, cch * NTH + qoff: cch * NTH + qoff + 512],
                          reads=spill_keys, writes=["qT_sb"])
                    S.dma("SYNC", kT_sb[:, cch * 1024:(cch + 1) * 1024],
                          khbm[:, cch * NTH + koff0: cch * NTH + koff0 + 1024],
                          reads=spill_keys, writes=["kT_sb"])
                S.dma("SYNC", vt_sb[:], vhbm[:, (koff0 // 128) * 1024:
                                              (koff0 // 128 + 8) * 1024],
                      reads=spill_keys, writes=["vt_sb"])

                for pair in range(8):
                    last_mm = {}
                    for m in range(8):
                        ktt = m
                        for hh in range(2):
                            h = pair * 2 + hh
                            cch = h // 2
                            rb = 64 * (h % 2)
                            sbk = hh
                            ob = 2 + hh
                            db_ = 4 + hh
                            fns = [
                                lambda e, o=pb[sbk][:, 0:512],
                                l=kT_sb[rb:rb + 64, cch * 1024 + m * 128: cch * 1024 + (m + 1) * 128],
                                r=qT_sb[rb:rb + 64, cch * 512: (cch + 1) * 512]:
                                e.matmul(o, l, r, start=True, stop=False),
                                lambda e, o=pb[sbk][:, 0:512],
                                r=mask_sb[:, (mset * 8 + m) * 512:(mset * 8 + m + 1) * 512]:
                                e.matmul(o, id_bf[:], r, start=False, stop=True),
                            ]
                            S.mmg(PB[sbk], fns, reads=["kT_sb", "qT_sb", "mask", "id"])
                            pti = hh
                            S.op("ACT", lambda e, o=pT[pti][:], i=pb[sbk][:, 0:512]:
                                 e.activation(o, i, AF.Exp, scale=cst[:, 0:1]),
                                 reads=[PB[sbk], "cst"], writes=[("pT", pti)])
                            tok_o = S.op("TE", lambda e, o=pb[ob][rb:rb + 64, 0:512],
                                         l=vt_sb[:, ktt * 1024 + h * 64: ktt * 1024 + h * 64 + 64],
                                         r=pT[pti][:], st=(m == 0), sp=(m == 7), tp=(0, rb):
                                         e.matmul(o, l, r, start=st, stop=sp, tile_position=tp),
                                         reads=[("pT", pti), "vt_sb"] +
                                               ([PB[ob]] if m == 0 else []),
                                         writes=[PB[ob]] if m == 0 else [])
                            tok_d = S.op("TE", lambda e, o=pb[db_][rb:rb + 64, 0:512],
                                         l=on64_bf[:], r=pT[pti][:], st=(m == 0), sp=(m == 7),
                                         tp=(0, rb):
                                         e.matmul(o, l, r, start=st, stop=sp, tile_position=tp),
                                         reads=[("pT", pti), "on64"] +
                                               ([PB[db_]] if m == 0 else []),
                                         writes=[PB[db_]] if m == 0 else [])
                            last_mm[ob] = tok_o
                            last_mm[db_] = tok_d
                    for bk, tk in last_mm.items():
                        S.lw[PB[bk]] = tk
                    for hh in range(2):
                        ob, db_ = 2 + hh, 4 + hh
                        rb = 64 * hh
                        S.op("DVE", lambda e, o=dnr[rb:rb + 64, :], i=pb[db_][rb:rb + 64, 0:512]:
                             e.reciprocal(o, i), reads=[PB[db_]], writes=[("dnr", hh)])
                        S.op("DVE", lambda e, o=outT_ch[rb:rb + 64, pair * 512:(pair + 1) * 512],
                             a=pb[ob][rb:rb + 64, 0:512], b=dnr[rb:rb + 64, :]:
                             e.tensor_mul(o, a, b),
                             reads=[PB[ob], ("dnr", hh)], writes=[("outT", pair, hh)])
                for tt in range(4):
                    trow = qc * 512 + tt * 128
                    S.dma("SYNC", xt2[:], xh[HALO + trow: HALO + trow + 128, :],
                          writes=["xt2"])
                    for dmc in range(2):
                        bi = dmc
                        fns = [lambda e, o=pb[bi][:, 0:512],
                               l=outT_ch[:, kch * 512 + tt * 128: kch * 512 + tt * 128 + 128],
                               r=wo_bf[:, kch * 1024 + dmc * 512: kch * 1024 + dmc * 512 + 512],
                               st=(kch == 0), sp=(kch == 7):
                               e.matmul(o, l, r, start=st, stop=sp) for kch in range(8)]
                        S.mmg(PB[bi], fns,
                              reads=[("outT", pr, hh_) for pr in range(8)
                                     for hh_ in range(2)] + ["wo"])
                        S.op("DVE", lambda e, o=y_sb[:, dmc * 512:(dmc + 1) * 512],
                             a=pb[bi][:, 0:512], b=xt2[:, dmc * 512:(dmc + 1) * 512]:
                             e.tensor_add(o, a, b),
                             reads=[PB[bi], "xt2"], writes=[("y", dmc)])
                        S.op("DVE", lambda e, o=y_sb[:, dmc * 512:(dmc + 1) * 512],
                             a=y_sb[:, dmc * 512:(dmc + 1) * 512],
                             b=bo_sb[:, dmc * 512:(dmc + 1) * 512]: e.tensor_add(o, a, b),
                             reads=[("y", dmc), "bo"], writes=[("y", dmc)])
                    S.dma("SYNC", x1_out[trow: trow + 128, :], y_sb[:],
                          reads=[("y", 0), ("y", 1)])
                    emit_ln(S, y_sb[:], lnx2[:], mst, sqs2[:], c3, c2,
                            [("y", 0), ("y", 1)], "lnx2", rk_extra=["cst"])
                    for half in range(2):
                        pti = half
                        fns = []
                        for q4 in range(4):
                            dch = half * 4 + q4
                            fns.append(lambda e, o=pt_bf[pti][:, q4 * 128:(q4 + 1) * 128],
                                       i=lnx2[:, dch * 128:(dch + 1) * 128]:
                                       e.transpose(o, i, id_bf[:]))
                        tok = S.op("TE", fns[0], reads=["lnx2", "id"], writes=[PT[pti]])
                        for f in fns[1:]:
                            tok = S.op("TE", f)
                        S.lw[PT[pti]] = tok
                        S.op("DVE", lambda e, pt=pt_bf[pti], half=half:
                             e.tensor_copy(lt_sb[:, half * 512: half * 512 + 512], pt[:, 0:512]),
                             reads=[PT[pti]], writes=[("lt", half)])
                    bi = 2
                    fns = [lambda e, o=pb[bi][:, 0:8],
                           l=lt_sb[:, kch * 128:(kch + 1) * 128],
                           r=wr_bf[:, kch * 8:(kch + 1) * 8],
                           st=(kch == 0), sp=(kch == 7):
                           e.matmul(o, l, r, start=st, stop=sp) for kch in range(8)]
                    S.mmg(PB[bi], fns, reads=[("lt", 0), ("lt", 1), "wrk"])
                    S.op("DVE", lambda e: e.tensor_add(lg_sb[:], pb[2][:, 0:8], br_sb[:]),
                         reads=[PB[2], "br"], writes=["lg"])
                    S.op("DVE", lambda e: e.reduce_max(m_sb[:, 0:1], lg_sb[:], axis=AX.X),
                         reads=["lg"], writes=["msb"])
                    S.op("DVE", lambda e: e.tensor_scalar(m_sb[:, 1:2], m_sb[:, 0:1], -1.0,
                                                          None, ALU.mult),
                         reads=["msb"], writes=["msb"])
                    S.op("ACT", lambda e: e.activation(lg_sb[:], lg_sb[:], AF.Exp,
                                                       bias=m_sb[:, 1:2]),
                         reads=["lg", "msb"], writes=["lg"])
                    S.op("DVE", lambda e: e.reduce_sum(m_sb[:, 0:1], lg_sb[:], axis=AX.X),
                         reads=["lg"], writes=["msb"])
                    S.op("DVE", lambda e: e.reciprocal(m_sb[:, 0:1], m_sb[:, 0:1]),
                         reads=["msb"], writes=["msb"])
                    S.op("DVE", lambda e: e.tensor_scalar(lg_sb[:], lg_sb[:], m_sb[:, 0:1],
                                                          None, ALU.mult),
                         reads=["lg", "msb"], writes=["lg"])
                    S.dma("SYNC", probs_out[trow: trow + 128, :], lg_sb[:], reads=["lg"])

            S.op("SYNC", lambda e: e.nop(), waits=S.now("QSP"), q="QSP", n=16)
            S.barrier()

        blk = ctx.enter_context(nc.Block())
        S.replay(blk, sems)

    return nc


FCH = [128] * 21 + [42]
FOF = [sum(FCH[:i]) for i in range(22)]


def hostB(inp, core, probs_all):
    sel64T = np.zeros((64, 8), np.float32)
    sel8 = np.zeros((8, 64), np.float32)
    for r in range(64):
        sel64T[r, r % 8] = 1.0
        sel8[r % 8, r] = 1.0
    pT = probs_all.T.reshape(8, 8, 2048)
    segs = [(s + core) % 8 for s in range(8)]
    pr = np.zeros((64, 2048), np.float32)
    for sp in range(8):
        for e in range(8):
            pr[sp * 8 + e] = pT[e, segs[sp]]
    bu = np.zeros((128, 4 * 22), np.float32)
    b_up = _f32(inp["L0_ffn_b_up"])
    for gg in range(4):
        pad = np.zeros(2816, np.float32)
        pad[:2730] = b_up[gg]
        bu[:, gg * 22:(gg + 1) * 22] = pad.reshape(22, 128).T
    return {
        "x1": np.zeros((NT, DM), np.float32),
        "probs_own": np.zeros((NT, 8), np.float32),
        "probsT_r": pr,
        "sel64T": sel64T, "sel8": sel8,
        "wup": _f32(inp["L0_ffn_W_up"]),
        "bu": bu,
        "wdn": _f32(inp["L0_ffn_W_down"]),
        "bd_row": _f32(inp["L0_ffn_b_down"]).reshape(1, 8 * 1024),
        "ones_row": np.ones((1, 128), np.float32),
        "wkvT": _kc(_f32(inp["L1_attn_Wkv_down"]).T),
        "identB": np.eye(128, dtype=np.float32),
        "constsB": np.tile(np.array(
            [0.5, 1e-12, 1e-5, 1.0 / 1024, 2048.0, 1.0, -1.0, 1e-6], np.float32),
            (128, 1)),
    }


def build_B():
    nc = bass.Bass("TRN2", target_bir_lowering=False)
    g = lambda n, s, dt=F32: nc.dram_tensor(n, s, dt, kind="ExternalInput").ap()
    x1_in = g("x1", [NT, DM])
    probs_own = g("probs_own", [NT, 8])
    probsT_r = g("probsT_r", [64, 2048])
    sel64T_in = g("sel64T", [64, 8])
    sel8_in = g("sel8", [8, 64])
    wup_in = g("wup", [4, 1024, 2730])
    bu_in = g("bu", [128, 88])
    wdn_in = g("wdn", [8, 2730, 1024])
    bd_in = g("bd_row", [1, 8192])
    ones_in = g("ones_row", [1, 128])
    wkvT_in = g("wkvT", [128, 8 * 64])
    id_in = g("identB", [128, 128])
    cst_in = g("constsB", [128, 8])

    x2_out = nc.dram_tensor("x2", [NT, DM], F32, kind="ExternalOutput").ap()
    lat_out = nc.dram_tensor("latent", [64, NT], F32, kind="ExternalOutput").ap()
    lo_dbg = nc.dram_tensor("lo_dbg", [64, 1], F32, kind="ExternalOutput").ap()

    lnT_hbm = nc.dram_tensor("lnT_hbm", [128, 8 * NT], BF16, kind="Internal").ap()
    yg_hbm = nc.dram_tensor("yg_hbm", [4 * NT, DM], F32, kind="Internal").ap()

    with ExitStack() as ctx:
        sb = lambda n, s, dt: ctx.enter_context(nc.sbuf_tensor(n, s, dt))
        psm = lambda n, s, dt=F32: ctx.enter_context(nc.psum_tensor(n, s, dt))

        cst = sb("cst", [128, 8], F32)
        id_bf = sb("idbf", [128, 128], BF16)
        ones_bf = sb("onesbf", [1, 128], BF16)
        bd_bf = sb("bdbf", [1, 8192], BF16)
        bu_sb = sb("busb", [128, 88], F32)
        mst = sb("mst", [128, 8], F32)
        w_tok = sb("w_tok", [128, 16 * 8], F32)
        wkv_bf = sb("wkvbf", [128, 8 * 64], BF16)

        pb = [psm(f"pb{i}", [128, 512]) for i in range(6)]
        pt_bf = [psm(f"ptb{i}", [128, 512], BF16) for i in range(2)]
        PB = [f"pb{i}" for i in range(6)]
        PT = ["pt0", "pt1"]

        S = Sched()
        sems = make_sems(nc, ctx)

        c2 = cst[:, 2:3]
        c3 = cst[:, 3:4]

        S.dma("SYNC", cst[:], cst_in, writes=["cst"])
        S.dma("SYNC", bu_sb[:], bu_in, writes=["bu"])

        with ExitStack() as pA:
            sbA = lambda n, s, dt: pA.enter_context(nc.sbuf_tensor(n, s, dt))
            stage = sbA("stage", [128, 2730], F32)
            xt = sbA("xt", [128, 1024], F32)
            sqs = sbA("sqs", [128, 1024], F32)
            lnx = sbA("lnx", [128, 1024], BF16)
            lnTt = sbA("lnTt", [128, 8 * 128], BF16)
            pts = sbA("pts", [64, 2048], F32)
            lo = sbA("lo", [64, 1], F32)
            hi = sbA("hi", [64, 1], F32)
            th = sbA("th", [64, 1], F32)
            cnt64 = sbA("cnt64", [64, 2048], F32)
            c64 = sbA("c64", [64, 2], F32)
            mask8 = sbA("mask8", [8, 2], BF16)
            maskR = sbA("maskR", [64, 2], F32)
            wsel = sbA("wsel", [8, 2048], BF16)
            mtmp = sbA("mtmp", [128, 8], F32)
            sel64T_sb = sbA("sel64Ts", [64, 8], F32)
            sel8_bf = sbA("sel8b", [8, 64], BF16)

            def conv_small(src_ap, dst_ap, rows, cols, key):
                S.dma("SYNC", stage[0:rows, 0:cols], src_ap, writes=["stage"])
                S.op("DVE", lambda e, d=dst_ap, s=stage[0:rows, 0:cols]:
                     e.tensor_copy(d, s), reads=["stage"], writes=[key])

            S.dma("SYNC", sel64T_sb[:], sel64T_in, writes=["sel64T"])
            S.dma("SYNC", pts[:], probsT_r, writes=["pts"])
            conv_small(id_in, id_bf[:], 128, 128, "id")
            conv_small(ones_in, ones_bf[:], 1, 128, "ones")
            conv_small(sel8_in, sel8_bf[:], 8, 64, "sel8")
            conv_small(wkvT_in, wkv_bf[:], 128, 512, "wkv")
            for off in range(0, 8192, 2048):
                conv_small(bd_in[:, off:off + 2048], bd_bf[:, off:off + 2048],
                           1, 2048, "bd")

            S.op("DVE", lambda e: e.memset(lo[:], 0.0), writes=["lo"])
            S.op("DVE", lambda e: e.memset(hi[:], 1.0), writes=["hi"])
            for it in range(40):
                S.op("DVE", lambda e: e.tensor_add(th[:], lo[:], hi[:]),
                     reads=["lo", "hi"], writes=["th"])
                S.op("DVE", lambda e: e.tensor_scalar(th[:], th[:], 0.5, None, ALU.mult),
                     reads=["th"], writes=["th"])
                S.op("DVE", lambda e: e.tensor_scalar(cnt64[:], pts[:], th[:], None, ALU.is_ge),
                     reads=["pts", "th"], writes=["cnt"])
                S.op("DVE", lambda e: e.reduce_sum(c64[:, 0:1], cnt64[:], axis=AX.X),
                     reads=["cnt"], writes=["c64"])
                S.mmg(PB[4], [lambda e: e.matmul(pb[4][0:8, 0:1], sel64T_sb[:], c64[:, 0:1],
                                                 start=True, stop=True)],
                      reads=["c64", "sel64T"])
                S.op("DVE", lambda e: e.tensor_scalar(mask8[:, 0:1], pb[4][0:8, 0:1],
                                                      2048.0, None, ALU.is_ge),
                     reads=[PB[4]], writes=["mask8"])
                S.op("DVE", lambda e: e.tensor_scalar(mask8[:, 1:2], pb[4][0:8, 0:1],
                                                      2048.0, None, ALU.is_lt),
                     reads=[PB[4]], writes=["mask8"])
                S.mmg(PB[5], [lambda e: e.matmul(pb[5][0:64, 0:2], sel8_bf[:], mask8[:],
                                                 start=True, stop=True)],
                      reads=["mask8", "sel8"])
                S.op("ACT", lambda e: e.activation(maskR[:], pb[5][0:64, 0:2], AF.Identity),
                     reads=[PB[5]], writes=["maskR"])
                S.op("DVE", lambda e: e.tensor_mul(c64[:, 0:1], th[:], maskR[:, 0:1]),
                     reads=["maskR", "th"], writes=["c64"])
                S.op("DVE", lambda e: e.scalar_tensor_tensor(
                    lo[:], lo[:], maskR[:, 1:2], c64[:, 0:1], ALU.mult, ALU.add),
                     reads=["maskR", "lo", "c64"], writes=["lo"])
                S.op("DVE", lambda e: e.tensor_mul(c64[:, 1:2], th[:], maskR[:, 1:2]),
                     reads=["maskR", "th"], writes=["c64b"])
                S.op("DVE", lambda e: e.scalar_tensor_tensor(
                    hi[:], hi[:], maskR[:, 0:1], c64[:, 1:2], ALU.mult, ALU.add),
                     reads=["maskR", "hi", "c64b"], writes=["hi"])
            S.dma("SYNC", lo_dbg, lo[:], reads=["lo"])
            S.op("DVE", lambda e: e.tensor_scalar(wsel[:], pts[0:8, :], lo[0:8, 0:1],
                                                  None, ALU.is_ge),
                 reads=["pts", "lo"], writes=["wsel"])
            for tt in range(16):
                S.op("TE", lambda e, o=pt_bf[tt % 2][0:128, 0:8],
                     i=wsel[:, tt * 128:(tt + 1) * 128]:
                     e.transpose(o, i, id_bf[0:8, 0:8]),
                     reads=["wsel", "id"], writes=[PT[tt % 2]])
                S.op("ACT", lambda e, o=mtmp[:], i=pt_bf[tt % 2][0:128, 0:8]:
                     e.activation(o, i, AF.Identity),
                     reads=[PT[tt % 2]], writes=["mtmp"])
                S.dma("SYNC", xt[:, 0:8], probs_own[tt * 128:(tt + 1) * 128, :],
                      writes=["xt"])
                S.op("DVE", lambda e, o=w_tok[:, tt * 8:(tt + 1) * 8], a=mtmp[:],
                     tt=tt: e.tensor_mul(o, a, xt[:, 0:8]),
                     reads=["mtmp", "xt"], writes=[("w_tok", tt)])

            for tt in range(16):
                S.dma("SYNC", xt[:], x1_in[tt * 128:(tt + 1) * 128, :], writes=["xt"])
                emit_ln(S, xt[:], lnx[:], mst, sqs[:], c3, c2, "xt", "lnx",
                        rk_extra=["cst"])
                for half in range(2):
                    pti = half
                    fns = []
                    for q4 in range(4):
                        dch = half * 4 + q4
                        fns.append(lambda e, o=pt_bf[pti][:, q4 * 128:(q4 + 1) * 128],
                                   i=lnx[:, dch * 128:(dch + 1) * 128]:
                                   e.transpose(o, i, id_bf[:]))
                    tok = S.op("TE", fns[0], reads=["lnx", "id"], writes=[PT[pti]])
                    for f in fns[1:]:
                        tok = S.op("TE", f)
                    S.lw[PT[pti]] = tok
                    S.op("DVE", lambda e, pt=pt_bf[pti], half=half:
                         e.tensor_copy(lnTt[:, half * 512: half * 512 + 512], pt[:, 0:512]),
                         reads=[PT[pti]], writes=[("lnTt", half)])
                for kch in range(8):
                    S.dma("SYNC", lnT_hbm[:, kch * NT + tt * 128: kch * NT + (tt + 1) * 128],
                          lnTt[:, kch * 128:(kch + 1) * 128],
                          reads=[("lnTt", kch // 4)], writes=[("lnhbm", tt, kch)])
            S.barrier()

        lnkeys = [("lnhbm", tt, kch) for tt in range(16) for kch in range(8)]
        with ExitStack() as pB:
            sbB = lambda n, s, dt: pB.enter_context(nc.sbuf_tensor(n, s, dt))
            wup_bf = sbB("wupbf", [128, 8 * 2730], BF16)
            wd_bf = {ee: sbB(f"wdbf{ee}", [128, 22 * 1024], BF16) for ee in range(2)}
            stg = sbB("stg", [128, 1408], F32)
            stg2 = sbB("stg2", [128, 1024], F32)
            midT = sbB("midT", [128, 22 * 512], BF16)
            lnT_sb = sbB("lnTsb", [128, 8 * 512], BF16)
            y_sb = sbB("ysb", [128, 4096], F32)

            for gg in range(4):
                for kch in range(8):
                    for c0_, c1_ in ((0, 1408), (1408, 2730)):
                        S.dma("SYNC", stg[:, 0:c1_ - c0_],
                              wup_in[gg, kch * 128:(kch + 1) * 128, c0_:c1_],
                              writes=["stg"])
                        S.op("DVE", lambda e, d=wup_bf[:, kch * 2730 + c0_: kch * 2730 + c1_],
                             s=stg[:, 0:c1_ - c0_]: e.tensor_copy(d, s),
                             reads=["stg"], writes=["wup"])
                for ee in range(2):
                    e_ = gg * 2 + ee
                    for mj in range(22):
                        mm = FCH[mj]
                        S.dma("ACT", stg2[0:mm, 0:1024],
                              wdn_in[e_, FOF[mj]:FOF[mj] + mm, :], writes=["stg2"])
                        S.op("DVE", lambda e, d=wd_bf[ee][0:mm, mj * 1024:(mj + 1) * 1024],
                             s=stg2[0:mm, 0:1024]: e.tensor_copy(d, s),
                             reads=["stg2"], writes=[("wd", ee)])
                for tc in range(4):
                    for kch in range(8):
                        S.dma("SYNC", lnT_sb[:, kch * 512:(kch + 1) * 512],
                              lnT_hbm[:, kch * NT + tc * 512: kch * NT + tc * 512 + 512],
                              reads=lnkeys, writes=["lnT_sb"])
                    for mj in range(22):
                        mm = FCH[mj]
                        bi = mj % 2
                        fns = [lambda e, o=pb[bi][0:mm, 0:512],
                               l=wup_bf[:, kch * 2730 + FOF[mj]: kch * 2730 + FOF[mj] + mm],
                               r=lnT_sb[:, kch * 512:(kch + 1) * 512],
                               st=(kch == 0), sp=(kch == 7):
                               e.matmul(o, l, r, start=st, stop=sp) for kch in range(8)]
                        S.mmg(PB[bi], fns, reads=["wup", "lnT_sb"])
                        S.op("ACT", lambda e, o=midT[0:mm, mj * 512:(mj + 1) * 512],
                             i=pb[bi][0:mm, 0:512], b=bu_sb[0:mm, gg * 22 + mj: gg * 22 + mj + 1]:
                             e.activation(o, i, AF.Silu, bias=b),
                             reads=[PB[bi], "bu"], writes=[("midT", mj)])
                    for ee in range(2):
                        e_ = gg * 2 + ee
                        for tt in range(4):
                            gt = tc * 4 + tt
                            for dmc in range(2):
                                bi = 2 + (tt * 2 + dmc) % 2
                                fns = []
                                for mj in range(22):
                                    mm = FCH[mj]
                                    fns.append(lambda e, o=pb[bi][:, 0:512],
                                               l=midT[0:mm, mj * 512 + tt * 128: mj * 512 + (tt + 1) * 128],
                                               r=wd_bf[ee][0:mm, mj * 1024 + dmc * 512: mj * 1024 + dmc * 512 + 512],
                                               st=(mj == 0): e.matmul(o, l, r, start=st, stop=False))
                                fns.append(lambda e, o=pb[bi][:, 0:512],
                                           l=ones_bf[:],
                                           r=bd_bf[:, e_ * 1024 + dmc * 512: e_ * 1024 + dmc * 512 + 512]:
                                           e.matmul(o, l, r, start=False, stop=True))
                                S.mmg(PB[bi], fns,
                                      reads=[("midT", mj_) for mj_ in range(22)] +
                                            [("wd", ee), "bd", "ones"])
                                wap = w_tok[:, gt * 8 + e_: gt * 8 + e_ + 1]
                                if ee == 0:
                                    S.op("DVE", lambda e, o=y_sb[:, (tt * 2 + dmc) * 512:
                                                                 (tt * 2 + dmc + 1) * 512],
                                         i=pb[bi][:, 0:512], w=wap:
                                         e.tensor_scalar(o, i, w, None, ALU.mult),
                                         reads=[PB[bi], ("w_tok", gt)],
                                         writes=[("y", tt, dmc)])
                                else:
                                    S.op("DVE", lambda e, o=y_sb[:, (tt * 2 + dmc) * 512:
                                                                 (tt * 2 + dmc + 1) * 512],
                                         i=pb[bi][:, 0:512], w=wap:
                                         e.scalar_tensor_tensor(o, i, w, o, ALU.mult, ALU.add),
                                         reads=[PB[bi], ("w_tok", gt), ("y", tt, dmc)],
                                         writes=[("y", tt, dmc)])
                    for tt in range(4):
                        gt = tc * 4 + tt
                        S.dma("SYNC", yg_hbm[gg * NT + gt * 128: gg * NT + (gt + 1) * 128, :],
                              y_sb[:, tt * 1024:(tt + 1) * 1024],
                              reads=[("y", tt, 0), ("y", tt, 1)],
                              writes=[("yg", gg, gt)])
            S.barrier()

        with ExitStack() as pC:
            sbC = lambda n, s, dt: pC.enter_context(nc.sbuf_tensor(n, s, dt))
            xt2 = sbC("xt2", [128, 1024], F32)
            yt = sbC("yt", [128, 4 * 1024], F32)
            sqs2 = sbC("sqs2", [128, 1024], F32)
            lnx2 = sbC("lnx2", [128, 1024], BF16)
            lnT2 = sbC("lnT2", [128, 8 * 128], BF16)
            latT = sbC("latT", [64, 2048], F32)

            ygkeys = [("yg", gg, gt) for gg in range(4) for gt in range(16)]
            for tt in range(16):
                S.dma("SYNC", xt2[:], x1_in[tt * 128:(tt + 1) * 128, :], writes=["xt2"])
                for gg in range(4):
                    S.dma("SYNC", yt[:, gg * 1024:(gg + 1) * 1024],
                          yg_hbm[gg * NT + tt * 128: gg * NT + (tt + 1) * 128, :],
                          reads=ygkeys, writes=[("ytl", gg)])
                for gg in range(4):
                    S.op("DVE", lambda e, a=xt2[:], b=yt[:, gg * 1024:(gg + 1) * 1024]:
                         e.tensor_add(a, a, b),
                         reads=[("ytl", gg), "xt2"], writes=["xt2"])
                S.dma("SYNC", x2_out[tt * 128:(tt + 1) * 128, :], xt2[:], reads=["xt2"])
                emit_ln(S, xt2[:], lnx2[:], mst, sqs2[:], c3, c2, "xt2", "lnx2",
                        rk_extra=["cst"])
                for half in range(2):
                    pti = half
                    fns = []
                    for q4 in range(4):
                        dch = half * 4 + q4
                        fns.append(lambda e, o=pt_bf[pti][:, q4 * 128:(q4 + 1) * 128],
                                   i=lnx2[:, dch * 128:(dch + 1) * 128]:
                                   e.transpose(o, i, id_bf[:]))
                    tok = S.op("TE", fns[0], reads=["lnx2", "id"], writes=[PT[pti]])
                    for f in fns[1:]:
                        tok = S.op("TE", f)
                    S.lw[PT[pti]] = tok
                    S.op("DVE", lambda e, pt=pt_bf[pti], half=half:
                         e.tensor_copy(lnT2[:, half * 512: half * 512 + 512], pt[:, 0:512]),
                         reads=[PT[pti]], writes=[("lnT2", half)])
                bi = 4
                fns = [lambda e, o=pb[bi][0:64, 0:128],
                       l=wkv_bf[:, kch * 64:(kch + 1) * 64],
                       r=lnT2[:, kch * 128:(kch + 1) * 128],
                       st=(kch == 0), sp=(kch == 7):
                       e.matmul(o, l, r, start=st, stop=sp) for kch in range(8)]
                S.mmg(PB[bi], fns, reads=[("lnT2", 0), ("lnT2", 1), "wkv"])
                S.op("DVE", lambda e, o=latT[:, tt * 128:(tt + 1) * 128],
                     i=pb[bi][0:64, 0:128]: e.tensor_copy(o, i),
                     reads=[PB[bi]], writes=[("latT", tt)])
            for i0 in range(0, 2048, 1024):
                S.dma("SYNC", lat_out[:, i0:i0 + 1024], latT[:, i0:i0 + 1024],
                      reads=[("latT", t_) for t_ in range(16)])
            S.op("SYNC", lambda e: e.nop(), waits=S.now("QSP", "QACT"), q="QSP", n=16)
            S.barrier()

        blk = ctx.enter_context(nc.Block())
        S.replay(blk, sems)

    return nc




def _gup_perm(gu):
    guT = gu.transpose(0, 2, 1)
    out = np.empty((8, 22, 128, 2048), np.float32)
    for mj in range(22):
        for gi in range(2):
            mjj = mj + 22 * gi
            blk = guT[:, :, mjj * 128:(mjj + 1) * 128]
            blk = blk.reshape(8, 8, 128, 128)
            out[:, mj, :, gi * 1024:(gi + 1) * 1024] = \
                blk.transpose(0, 2, 1, 3).reshape(8, 128, 1024)
    return np.ascontiguousarray(out)

def hostCD(inp, core, latT_full):
    return {
        "x2": np.zeros((NT, DM), np.float32),
        "latT": np.ascontiguousarray(latT_full),
        "wqTl1": _kc(_f32(inp["L1_attn_Wq"]).T),
        "wkupT": np.ascontiguousarray(_f32(inp["L1_attn_Wk_up"]).T),
        "wvupT": np.ascontiguousarray(_f32(inp["L1_attn_Wv_up"]).T),
        "woTl1": _kc(_f32(inp["L1_attn_Wo"]).T),
        "wnorm_bc": np.broadcast_to(_f32(inp["L1_ffn_norm_weight"]), (128, 1024)).copy(),
        "rwT": _kc(_f32(inp["L1_ffn_router_weight"]).T),
        "gupT": _gup_perm(_f32(inp["L1_ffn_gate_up_weight"])),
        "dnT": np.ascontiguousarray(_f32(inp["L1_ffn_down_weight"]).transpose(0, 2, 1)),
        "alpha_bc": np.broadcast_to(_f32(inp["L1_ffn_residual_alpha"]).reshape(1, 1),
                                    (128, 1)).copy(),
        "ones1": np.ones((1, 128), np.float32),
        "onescol": np.ones((128, 1), np.float32),
        "identC": np.eye(128, dtype=np.float32),
        "constsC": np.tile(np.array(
            [0.0625, 1e-12, 1e-5, 1.0 / 1024, 1e-6, 1.0, -1.0, -1e9], np.float32),
            (128, 1)),
    }


def build_CD():
    nc = bass.Bass("TRN2", target_bir_lowering=False)
    g = lambda n, s, dt=F32: nc.dram_tensor(n, s, dt, kind="ExternalInput").ap()
    x2_in = g("x2", [NT, DM])
    latT_in = g("latT", [64, 4096])
    wqT_in = g("wqTl1", [128, 8 * 2048])
    wkupT_in = g("wkupT", [64, 2048])
    wvupT_in = g("wvupT", [64, 2048])
    woT_in = g("woTl1", [128, 16 * 1024])
    wnorm_in = g("wnorm_bc", [128, 1024])
    rwT_in = g("rwT", [128, 64])
    gupT_in = g("gupT", [8, 22, 128, 2048])
    dnT_in = g("dnT", [8, 2816, 1024])
    alpha_in = g("alpha_bc", [128, 1])
    ones1_in = g("ones1", [1, 128])
    onescol_in = g("onescol", [128, 1])
    id_in = g("identC", [128, 128])
    cst_in = g("constsC", [128, 8])

    out_f = nc.dram_tensor("out", [NT, DM], F32, kind="ExternalOutput").ap()
    x3_hbm = nc.dram_tensor("x3_hbm", [NT, DM], F32, kind="Internal").ap()

    with ExitStack() as ctx:
        sb = lambda n, s, dt: ctx.enter_context(nc.sbuf_tensor(n, s, dt))
        psm = lambda n, s, dt=F32: ctx.enter_context(nc.psum_tensor(n, s, dt))

        cst = sb("cst", [128, 8], F32)
        id_bf = sb("idbf", [128, 128], BF16)
        ones1_bf = sb("ones1bf", [1, 128], BF16)
        onescol_bf = sb("onescolbf", [128, 1], BF16)
        ones1_f = sb("ones1f", [1, 128], F32)
        mst = sb("mst", [128, 8], F32)
        alpha_sb = sb("alphasb", [128, 1], F32)
        w2_tok = sb("w2tok", [128, 16 * 8], F32)

        pb = [psm(f"pb{i}", [128, 512]) for i in range(6)]
        pt_bf = [psm(f"ptb{i}", [128, 512], BF16) for i in range(2)]
        PB = [f"pb{i}" for i in range(6)]
        PT = ["pt0", "pt1"]

        S = Sched()
        sems = make_sems(nc, ctx)
        c1 = cst[:, 1:2]
        c2 = cst[:, 2:3]
        c3 = cst[:, 3:4]
        c4 = cst[:, 4:5]

        S.dma("SYNC", cst[:], cst_in, writes=["cst"])
        S.dma("SYNC", alpha_sb[:], alpha_in, writes=["alpha"])
        S.op("DVE", lambda e: e.tensor_scalar(alpha_sb[:], alpha_sb[:], 2.0, None, ALU.min),
             reads=["alpha"], writes=["alpha"])
        S.op("DVE", lambda e: e.tensor_scalar(alpha_sb[:], alpha_sb[:], -0.5, None, ALU.max),
             reads=["alpha"], writes=["alpha"])

        with ExitStack() as pC:
            sbC = lambda n, s, dt: pC.enter_context(nc.sbuf_tensor(n, s, dt))
            stage = sbC("stage", [128, 1024], F32)
            lnx2T = sbC("lnx2T", [128, 8 * 2048], BF16)
            lat_bf = sbC("latbf", [64, 4096], BF16)
            wkup_bf = sbC("wkupbf", [64, 2048], BF16)
            wvup_bf = sbC("wvupbf", [64, 2048], BF16)
            woh_bf = sbC("wohbf", [128, 2 * 1024], BF16)
            wqh_bf = sbC("wqhbf", [128, 8 * 256], BF16)
            qT_h = sbC("qTh", [128, 2 * 2048], BF16)
            kT_h = sbC("kTh", [128, 2 * 4096], BF16)
            v_h = sbC("vh", [128, 32 * 256], BF16)
            outT_h = sbC("outTh", [128, 2 * 2048], BF16)
            y_acc = sbC("yaccC", [128, 16 * 1024], F32)
            pT = [sbC(f"pTc{i}", [128, 512], BF16) for i in range(2)]
            dnr = sbC("dnrc", [1, 512], F32)
            rb_f = sbC("rbf", [128, 512], F32)
            xt = sbC("xtc", [128, 1024], F32)
            sqs = sbC("sqsc", [128, 1024], F32)
            lnx = sbC("lnxc", [128, 1024], BF16)
            y_sb = sbC("ysbc", [128, 1024], F32)

            def conv_in(src_ap, dst_ap, rows, width, key, eng="SYNC"):
                for off in range(0, width, 1024):
                    wdt = min(1024, width - off)
                    S.dma(eng, stage[0:rows, 0:wdt], src_ap[0:rows, off:off + wdt],
                          writes=["stage"])
                    S.op("DVE", lambda e, d=dst_ap[0:rows, off:off + wdt],
                         s=stage[0:rows, 0:wdt]: e.tensor_copy(d, s),
                         reads=["stage"], writes=[key])

            conv_in(id_in, id_bf[:], 128, 128, "id")
            conv_in(ones1_in, ones1_bf[:], 1, 128, "ones1")
            conv_in(onescol_in, onescol_bf[:], 128, 1, "onescol")
            S.dma("SYNC", ones1_f[:], ones1_in, writes=["ones1f"])
            conv_in(latT_in, lat_bf[:], 64, 4096, "lat")
            conv_in(wkupT_in, wkup_bf[:], 64, 2048, "wkup")
            conv_in(wvupT_in, wvup_bf[:], 64, 2048, "wvup")

            for tt in range(16):
                S.dma("SYNC", xt[:], x2_in[tt * 128:(tt + 1) * 128, :], writes=["xt"])
                emit_ln(S, xt[:], lnx[:], mst, sqs[:], c3, c2, "xt", "lnx",
                        rk_extra=["cst"])
                for half in range(2):
                    pti = half
                    fns = []
                    for q4 in range(4):
                        dch = half * 4 + q4
                        fns.append(lambda e, o=pt_bf[pti][:, q4 * 128:(q4 + 1) * 128],
                                   i=lnx[:, dch * 128:(dch + 1) * 128]:
                                   e.transpose(o, i, id_bf[:]))
                    tok = S.op("TE", fns[0], reads=["lnx", "id"], writes=[PT[pti]])
                    for f in fns[1:]:
                        tok = S.op("TE", f)
                    S.lw[PT[pti]] = tok

                    def ev(e, pt=pt_bf[pti], half=half, tt=tt):
                        src = pt[:, 0:512].rearrange("p (a b) -> p a b", a=4)
                        d = lnx2T[:].rearrange("p (a b) -> p a b", a=8, b=2048)[
                            :, half * 4: half * 4 + 4, tt * 128: tt * 128 + 128]
                        return e.tensor_copy(d, src)
                    S.op("DVE", ev, reads=[PT[pti]], writes=[("lnx2T", tt)])
            ln2keys = [("lnx2T", t_) for t_ in range(16)]

            for h in range(8):
                conv_in(woT_in[:, 2 * h * 1024:(2 * h + 2) * 1024], woh_bf[:],
                        128, 2048, "wo")
                for kch in range(8):
                    S.dma("SYNC", stage[:, 0:256],
                          wqT_in[:, kch * 2048 + h * 256: kch * 2048 + (h + 1) * 256],
                          writes=["stage"])
                    S.op("DVE", lambda e, d=wqh_bf[:, kch * 256:(kch + 1) * 256],
                         s=stage[:, 0:256]: e.tensor_copy(d, s),
                         reads=["stage"], writes=["wqh"])
                for mch in range(2):
                    for nc4 in range(4):
                        bi = (mch * 4 + nc4) % 2
                        fns = [lambda e, o=pb[bi][:, 0:512],
                               l=wqh_bf[:, kch * 256 + mch * 128: kch * 256 + mch * 128 + 128],
                               r=lnx2T[:, kch * 2048 + nc4 * 512: kch * 2048 + (nc4 + 1) * 512],
                               st=(kch == 0), sp=(kch == 7):
                               e.matmul(o, l, r, start=st, stop=sp) for kch in range(8)]
                        S.mmg(PB[bi], fns, reads=ln2keys + ["wqh"])
                        S.op("ACT", lambda e, o=qT_h[:, mch * 2048 + nc4 * 512:
                                                     mch * 2048 + (nc4 + 1) * 512],
                             i=pb[bi][:, 0:512]: e.activation(o, i, AF.Identity),
                             reads=[PB[bi]], writes=["qT_h"])
                for mch in range(2):
                    for nc8 in range(8):
                        bi = 2 + (mch * 8 + nc8) % 2
                        S.mmg(PB[bi], [lambda e, o=pb[bi][:, 0:512],
                                       l=wkup_bf[0:64, h * 256 + mch * 128: h * 256 + mch * 128 + 128],
                                       r=lat_bf[0:64, nc8 * 512:(nc8 + 1) * 512]:
                                       e.matmul(o, l, r, start=True, stop=True)],
                              reads=["wkup", "lat"])
                        S.op("ACT", lambda e, o=kT_h[:, mch * 4096 + nc8 * 512:
                                                     mch * 4096 + (nc8 + 1) * 512],
                             i=pb[bi][:, 0:512]: e.activation(o, i, AF.Identity),
                             reads=[PB[bi]], writes=["kT_h"])
                for ktt in range(32):
                    bi = 4 + ktt % 2
                    S.mmg(PB[bi], [lambda e, o=pb[bi][:, 0:256],
                                   l=lat_bf[0:64, ktt * 128:(ktt + 1) * 128],
                                   r=wvup_bf[0:64, h * 256:(h + 1) * 256]:
                                   e.matmul(o, l, r, start=True, stop=True)],
                          reads=["lat", "wvup"])
                    S.op("ACT", lambda e, o=v_h[:, ktt * 256:(ktt + 1) * 256],
                         i=pb[bi][:, 0:256]: e.activation(o, i, AF.Identity),
                         reads=[PB[bi]], writes=["v_h"])
                for qc in range(4):
                    for m in range(32):
                        sbk = m % 2
                        fns = [lambda e, o=pb[sbk][:, 0:512],
                               l=kT_h[:, mch * 4096 + m * 128: mch * 4096 + (m + 1) * 128],
                               r=qT_h[:, mch * 2048 + qc * 512: mch * 2048 + (qc + 1) * 512],
                               st=(mch == 0), sp=(mch == 1):
                               e.matmul(o, l, r, start=st, stop=sp) for mch in range(2)]
                        S.mmg(PB[sbk], fns, reads=["kT_h", "qT_h"])
                        pti = m % 2
                        S.op("ACT", lambda e, o=pT[pti][:], i=pb[sbk][:, 0:512]:
                             e.activation(o, i, AF.Exp, scale=cst[:, 0:1]),
                             reads=[PB[sbk], "cst"], writes=[("pT", pti)])
                        for dm in range(2):
                            S.op("TE", lambda e, o=pb[2 + dm][:, 0:512],
                                 l=v_h[:, m * 256 + dm * 128: m * 256 + (dm + 1) * 128],
                                 r=pT[pti][:], st=(m == 0), sp=(m == 31):
                                 e.matmul(o, l, r, start=st, stop=sp),
                                 reads=[("pT", pti), "v_h"] +
                                       ([PB[2 + dm]] if m == 0 else []),
                                 writes=[PB[2 + dm]] if m == 0 else [])
                        tok_d = S.op("TE", lambda e, o=pb[4][0:1, 0:512],
                                     l=onescol_bf[:], r=pT[pti][:], st=(m == 0), sp=(m == 31):
                                     e.matmul(o, l, r, start=st, stop=sp),
                                     reads=[("pT", pti), "onescol"] +
                                           ([PB[4]] if m == 0 else []),
                                     writes=[PB[4]] if m == 0 else [])
                    S.lw[PB[2]] = S.now("TE")[0]
                    S.lw[PB[3]] = S.now("TE")[0]
                    S.lw[PB[4]] = tok_d
                    S.op("DVE", lambda e: e.reciprocal(dnr[:], pb[4][0:1, 0:512]),
                         reads=[PB[4]], writes=["dnr"])
                    S.mmg(PB[5], [lambda e: e.matmul(pb[5][:, 0:512], ones1_f[:], dnr[:],
                                                     start=True, stop=True)],
                          reads=["dnr", "ones1f"])
                    S.op("ACT", lambda e: e.activation(rb_f[:], pb[5][:, 0:512], AF.Identity),
                         reads=[PB[5]], writes=["rb_f"])
                    for dm in range(2):
                        S.op("DVE", lambda e, o=outT_h[:, dm * 2048 + qc * 512:
                                                       dm * 2048 + (qc + 1) * 512],
                             a=pb[2 + dm][:, 0:512], dm=dm: e.tensor_mul(o, a, rb_f[:]),
                             reads=[PB[2 + dm], "rb_f"], writes=[("outT", dm, qc)])
                okeys = [("outT", d_, q_) for d_ in range(2) for q_ in range(4)]
                for tt in range(16):
                    for dmc in range(2):
                        bi = dmc
                        fns = [lambda e, o=pb[bi][:, 0:512],
                               l=outT_h[:, dm_ * 2048 + tt * 128: dm_ * 2048 + tt * 128 + 128],
                               r=woh_bf[:, dm_ * 1024 + dmc * 512:
                                        dm_ * 1024 + dmc * 512 + 512],
                               st=(dm_ == 0), sp=(dm_ == 1):
                               e.matmul(o, l, r, start=st, stop=sp) for dm_ in range(2)]
                        S.mmg(PB[bi], fns, reads=okeys + ["wo"])
                        ya = y_acc[:, tt * 1024 + dmc * 512: tt * 1024 + dmc * 512 + 512]
                        if h == 0:
                            S.op("ACT", lambda e, o=ya, i=pb[bi][:, 0:512]:
                                 e.activation(o, i, AF.Identity),
                                 reads=[PB[bi]], writes=[("yac", tt, dmc)])
                        else:
                            S.op("DVE", lambda e, o=ya, i=pb[bi][:, 0:512]:
                                 e.tensor_add(o, i, o),
                                 reads=[PB[bi], ("yac", tt, dmc)],
                                 writes=[("yac", tt, dmc)])
            for tt in range(16):
                S.dma("SYNC", xt[:], x2_in[tt * 128:(tt + 1) * 128, :], writes=["xt"])
                for dmc in range(2):
                    S.op("DVE", lambda e, o=y_sb[:, dmc * 512:(dmc + 1) * 512],
                         a=y_acc[:, tt * 1024 + dmc * 512: tt * 1024 + dmc * 512 + 512],
                         b=xt[:, dmc * 512:(dmc + 1) * 512]: e.tensor_add(o, a, b),
                         reads=[("yac", tt, dmc), "xt"], writes=[("y", dmc)])
                S.dma("SYNC", x3_hbm[tt * 128:(tt + 1) * 128, :], y_sb[:],
                      reads=[("y", 0), ("y", 1)], writes=[("x3", tt)])
            S.barrier()

        x3keys = [("x3", t_) for t_ in range(16)]
        with ExitStack() as pD:
            sbD = lambda n, s, dt: pD.enter_context(nc.sbuf_tensor(n, s, dt))
            stage = sbD("stageD", [128, 2048], F32)
            stg2 = sbD("stg2D", [128, 2048], F32)
            normT = sbD("normT", [128, 8 * 2048], BF16)
            hidT = sbD("hidT", [128, 22 * 1024], BF16)
            dn_bf = sbD("dnbf", [128, 22 * 1024], BF16)
            gup_bf = sbD("gupbf", [128, 2 * 8 * 128], BF16)
            y_acc = sbD("yacc", [128, 16 * 1024], BF16)
            wn_sb = sbD("wnsb", [128, 1024], F32)
            rw_bf = sbD("rwbf", [128, 64], BF16)
            xt3 = sbD("xt3", [128, 1024], F32)
            sqs3 = sbD("sqs3", [128, 1024], F32)
            lnx3 = sbD("lnx3", [128, 1024], BF16)
            nrm = sbD("nrm", [128, 1024], BF16)
            lg = sbD("lgD", [128, 8], F32)
            msD = sbD("msD", [128, 4], F32)
            prb = sbD("prb", [128, 8], F32)
            gtmp = sbD("gtmp", [128, 1024], BF16)
            utmp = sbD("utmp", [128, 1024], BF16)
            sctmp = sbD("sctmp", [128, 512], BF16)
            lnf = sbD("lnf", [128, 1024], F32)
            moef = sbD("moef", [128, 1024], F32)

            S.dma("SYNC", wn_sb[:], wnorm_in, writes=["wn"])
            S.dma("SYNC", stage[:, 0:64], rwT_in, writes=["stageD"])
            S.op("DVE", lambda e: e.tensor_copy(rw_bf[:], stage[:, 0:64]),
                 reads=["stageD"], writes=["rw"])

            for tt in range(16):
                S.dma("SYNC", xt3[:], x3_hbm[tt * 128:(tt + 1) * 128, :],
                      reads=x3keys, writes=["xt3"])
                emit_ln(S, xt3[:], lnx3[:], mst, sqs3[:], c3, c2, "xt3", "lnx3",
                        rk_extra=["cst"])
                S.op("ACT", lambda e: e.activation(sqs3[:], lnx3[:], AF.Square),
                     reads=["lnx3"], writes=["sqk"])
                S.op("DVE", lambda e: e.reduce_sum(msD[:, 0:1], sqs3[:], axis=AX.X),
                     reads=["sqk"], writes=["msD"])
                S.op("DVE", lambda e: e.tensor_scalar(msD[:, 0:1], msD[:, 0:1], c3, None,
                                                      ALU.mult),
                     reads=["msD", "cst"], writes=["msD"])
                S.op("ACT", lambda e: e.activation(msD[:, 1:2], msD[:, 0:1], AF.Sqrt,
                                                   bias=c4),
                     reads=["msD", "cst"], writes=["msD"])
                S.op("DVE", lambda e: e.reciprocal(msD[:, 1:2], msD[:, 1:2]),
                     reads=["msD"], writes=["msD"])
                S.op("DVE", lambda e: e.tensor_scalar(gtmp[:], lnx3[:], msD[:, 1:2], None,
                                                      ALU.mult),
                     reads=["lnx3", "msD"], writes=["gt"])
                S.op("DVE", lambda e: e.tensor_mul(nrm[:], gtmp[:], wn_sb[:]),
                     reads=["gt", "wn"], writes=["nrm"])
                for half in range(2):
                    pti = half
                    fns = []
                    for q4 in range(4):
                        dch = half * 4 + q4
                        fns.append(lambda e, o=pt_bf[pti][:, q4 * 128:(q4 + 1) * 128],
                                   i=nrm[:, dch * 128:(dch + 1) * 128]:
                                   e.transpose(o, i, id_bf[:]))
                    tok = S.op("TE", fns[0], reads=["nrm", "id"], writes=[PT[pti]])
                    for f in fns[1:]:
                        tok = S.op("TE", f)
                    S.lw[PT[pti]] = tok

                    def ev2(e, pt=pt_bf[pti], half=half, tt=tt):
                        src = pt[:, 0:512].rearrange("p (a b) -> p a b", a=4)
                        d = normT[:].rearrange("p (a b) -> p a b", a=8, b=2048)[
                            :, half * 4: half * 4 + 4, tt * 128: tt * 128 + 128]
                        return e.tensor_copy(d, src)
                    S.op("DVE", ev2, reads=[PT[pti]], writes=[("normT", tt)])
                bi = 4
                fns = [lambda e, o=pb[bi][:, 0:8],
                       l=normT[:].rearrange("p (a b) -> p a b", a=8, b=2048)[
                           :, kch, tt * 128: tt * 128 + 128],
                       r=rw_bf[:, kch * 8:(kch + 1) * 8],
                       st=(kch == 0), sp=(kch == 7):
                       e.matmul(o, l, r, start=st, stop=sp) for kch in range(8)]
                S.mmg(PB[bi], fns, reads=[("normT", tt), "rw"])
                S.op("ACT", lambda e: e.activation(lg[:], pb[4][:, 0:8], AF.Identity),
                     reads=[PB[4]], writes=["lg"])
                S.op("DVE", lambda e: e.reduce_max(msD[:, 2:3], lg[:], axis=AX.X),
                     reads=["lg"], writes=["msD"])
                S.op("DVE", lambda e: e.tensor_scalar(msD[:, 3:4], msD[:, 2:3], -1.0, None,
                                                      ALU.mult),
                     reads=["msD"], writes=["msD"])
                S.op("ACT", lambda e: e.activation(prb[:], lg[:], AF.Exp, bias=msD[:, 3:4]),
                     reads=["lg", "msD"], writes=["prb"])
                S.op("DVE", lambda e: e.reduce_sum(msD[:, 2:3], prb[:], axis=AX.X),
                     reads=["prb"], writes=["msD"])
                S.op("DVE", lambda e: e.reciprocal(msD[:, 2:3], msD[:, 2:3]),
                     reads=["msD"], writes=["msD"])
                S.op("DVE", lambda e: e.tensor_scalar(prb[:], prb[:], msD[:, 2:3], None,
                                                      ALU.mult),
                     reads=["prb", "msD"], writes=["prb"])
                S.op("DVE", lambda e: e.reduce_max(msD[:, 2:3], prb[:], axis=AX.X),
                     reads=["prb"], writes=["msD"])
                S.op("DVE", lambda e: e.tensor_scalar(lg[:], prb[:], msD[:, 2:3], None,
                                                      ALU.is_ge),
                     reads=["prb", "msD"], writes=["lg"])
                S.op("DVE", lambda e: e.scalar_tensor_tensor(sqs3[:, 0:8], lg[:],
                                                             cst[:, 7:8], prb[:],
                                                             ALU.mult, ALU.add),
                     reads=["lg", "prb", "cst"], writes=["sqk"])
                S.op("DVE", lambda e: e.reduce_max(msD[:, 3:4], sqs3[:, 0:8], axis=AX.X),
                     reads=["sqk"], writes=["msD"])
                S.op("DVE", lambda e: e.tensor_scalar(sqs3[:, 8:16], sqs3[:, 0:8],
                                                      msD[:, 3:4], None, ALU.is_ge),
                     reads=["sqk", "msD"], writes=["sqk"])
                S.op("DVE", lambda e: e.tensor_add(lg[:], lg[:], sqs3[:, 8:16]),
                     reads=["lg", "sqk"], writes=["lg"])
                S.op("DVE", lambda e: e.tensor_mul(prb[:], prb[:], lg[:]),
                     reads=["prb", "lg"], writes=["prb"])
                S.op("DVE", lambda e: e.reduce_sum(msD[:, 2:3], prb[:], axis=AX.X),
                     reads=["prb"], writes=["msD"])
                S.op("DVE", lambda e: e.reciprocal(msD[:, 2:3], msD[:, 2:3]),
                     reads=["msD"], writes=["msD"])
                S.op("DVE", lambda e, o=w2_tok[:, tt * 8:(tt + 1) * 8]:
                     e.tensor_scalar(o, prb[:], msD[:, 2:3], None, ALU.mult),
                     reads=["prb", "msD"], writes=[("w2", tt)])

            nkeys = [("normT", t_) for t_ in range(16)]
            for e_ in range(8):
                for mj in range(22):
                    S.dma("ACT", stg2[:, 0:1024],
                          dnT_in[e_, mj * 128:(mj + 1) * 128, :], writes=["stg2D"])
                    S.op("DVE", lambda e, d=dn_bf[:, mj * 1024:(mj + 1) * 1024],
                         s=stg2[:, 0:1024]: e.tensor_copy(d, s),
                         reads=["stg2D"], writes=["dn"])
                for tc2 in range(2):
                    for mj in range(22):
                        S.dma("SYNC", stage[:, 0:2048], gupT_in[e_, mj, :, :],
                              writes=["stageD"])
                        S.op("DVE", lambda e, d=gup_bf[:, 0:2048],
                             s=stage[:, 0:2048]: e.tensor_copy(d, s),
                             reads=["stageD"], writes=[("gup", 0), ("gup", 1)])
                        for nc2 in range(2):
                            tglob = tc2 * 1024 + nc2 * 512
                            big, biu = 0, 1
                            fns = [lambda e, o=pb[big][:, 0:512],
                                   l=gup_bf[:, kch * 128:(kch + 1) * 128],
                                   r=normT[:, kch * 2048 + tglob: kch * 2048 + tglob + 512],
                                   st=(kch == 0), sp=(kch == 7):
                                   e.matmul(o, l, r, start=st, stop=sp) for kch in range(8)]
                            S.mmg(PB[big], fns, reads=nkeys + [("gup", 0)])
                            fns = [lambda e, o=pb[biu][:, 0:512],
                                   l=gup_bf[:, 1024 + kch * 128: 1024 + (kch + 1) * 128],
                                   r=normT[:, kch * 2048 + tglob: kch * 2048 + tglob + 512],
                                   st=(kch == 0), sp=(kch == 7):
                                   e.matmul(o, l, r, start=st, stop=sp) for kch in range(8)]
                            S.mmg(PB[biu], fns, reads=nkeys + [("gup", 1)])
                            S.op("ACT", lambda e, o=gtmp[:, nc2 * 512:(nc2 + 1) * 512],
                                 i=pb[big][:, 0:512]: e.activation(o, i, AF.Silu),
                                 reads=[PB[big]], writes=["gt"])
                            S.op("ACT", lambda e, o=utmp[:, nc2 * 512:(nc2 + 1) * 512],
                                 i=pb[biu][:, 0:512]: e.activation(o, i, AF.Identity),
                                 reads=[PB[biu]], writes=["ut"])
                        S.op("DVE", lambda e, o=hidT[:, mj * 1024:(mj + 1) * 1024],
                             a=gtmp[:, 0:1024], b=utmp[:, 0:1024]: e.tensor_mul(o, a, b),
                             reads=["gt", "ut"], writes=[("hid", mj)])
                    for tt4 in range(8):
                        gt_ = tc2 * 8 + tt4
                        for dmc in range(2):
                            bi = 4 + dmc
                            fns = [lambda e, o=pb[bi][:, 0:512],
                                   l=hidT[:, mj * 1024 + tt4 * 128: mj * 1024 + tt4 * 128 + 128],
                                   r=dn_bf[:, mj * 1024 + dmc * 512: mj * 1024 + dmc * 512 + 512],
                                   st=(mj == 0), sp=(mj == 21):
                                   e.matmul(o, l, r, start=st, stop=sp) for mj in range(22)]
                            S.mmg(PB[bi], fns,
                                  reads=[("hid", mj_) for mj_ in range(22)] + ["dn"])
                            wap = w2_tok[:, gt_ * 8 + e_: gt_ * 8 + e_ + 1]
                            if e_ == 0:
                                S.op("DVE", lambda e, o=y_acc[:, gt_ * 1024 + dmc * 512:
                                                             gt_ * 1024 + dmc * 512 + 512],
                                     i=pb[bi][:, 0:512], w=wap:
                                     e.tensor_scalar(o, i, w, None, ALU.mult),
                                     reads=[PB[bi], ("w2", gt_)], writes=[("ya", gt_, dmc)])
                            else:
                                S.op("DVE", lambda e, o=sctmp[:], i=pb[bi][:, 0:512], w=wap:
                                     e.tensor_scalar(o, i, w, None, ALU.mult),
                                     reads=[PB[bi], ("w2", gt_)], writes=["sctmp"])
                                S.op("DVE", lambda e, o=y_acc[:, gt_ * 1024 + dmc * 512:
                                                             gt_ * 1024 + dmc * 512 + 512]:
                                     e.tensor_add(o, o, sctmp[:]),
                                     reads=["sctmp", ("ya", gt_, dmc)],
                                     writes=[("ya", gt_, dmc)])

            for tt in range(16):
                S.dma("SYNC", xt3[:], x3_hbm[tt * 128:(tt + 1) * 128, :],
                      reads=x3keys, writes=["xt3"])
                mk = "mstk"
                S.op("DVE", lambda e: e.reduce_sum(mst[:, 0:1], xt3[:], axis=AX.X),
                     reads=["xt3"], writes=[mk])
                S.op("ACT", lambda e: e.activation(sqs3[:], xt3[:], AF.Square),
                     reads=["xt3"], writes=["sqk"])
                S.op("DVE", lambda e: e.reduce_sum(mst[:, 1:2], sqs3[:], axis=AX.X),
                     reads=["sqk"], writes=[mk])
                S.op("DVE", lambda e: e.tensor_scalar(mst[:, 2:3], mst[:, 0:1], c3, None, ALU.mult),
                     reads=[mk, "cst"], writes=[mk])
                S.op("DVE", lambda e: e.tensor_scalar(mst[:, 3:4], mst[:, 1:2], c3, None, ALU.mult),
                     reads=[mk], writes=[mk])
                S.op("DVE", lambda e: e.tensor_mul(mst[:, 4:5], mst[:, 2:3], mst[:, 2:3]),
                     reads=[mk], writes=[mk])
                S.op("DVE", lambda e: e.tensor_sub(mst[:, 5:6], mst[:, 3:4], mst[:, 4:5]),
                     reads=[mk], writes=[mk])
                S.op("ACT", lambda e: e.activation(mst[:, 6:7], mst[:, 5:6], AF.Sqrt, bias=c2),
                     reads=[mk, "cst"], writes=[mk])
                S.op("DVE", lambda e: e.reciprocal(mst[:, 6:7], mst[:, 6:7]),
                     reads=[mk], writes=[mk])
                S.op("DVE", lambda e: e.tensor_mul(mst[:, 7:8], mst[:, 2:3], mst[:, 6:7]),
                     reads=[mk], writes=[mk])
                S.op("DVE", lambda e: e.tensor_scalar(mst[:, 7:8], mst[:, 7:8], -1.0, None, ALU.mult),
                     reads=[mk], writes=[mk])
                S.op("ACT", lambda e: e.activation(lnf[:], xt3[:], AF.Identity,
                                                   bias=mst[:, 7:8], scale=mst[:, 6:7]),
                     reads=[mk, "xt3"], writes=["lnf"])
                S.op("DVE", lambda e, o=moef[:], i=y_acc[:, tt * 1024:(tt + 1) * 1024]:
                     e.tensor_scalar(o, i, alpha_sb[:, 0:1], None, ALU.mult),
                     reads=[("ya", tt, 0), ("ya", tt, 1), "alpha"], writes=["moef"])
                S.op("DVE", lambda e: e.tensor_add(lnf[:], lnf[:], xt3[:]),
                     reads=["lnf", "xt3"], writes=["lnf"])
                S.op("DVE", lambda e: e.tensor_add(lnf[:], lnf[:], moef[:]),
                     reads=["lnf", "moef"], writes=["lnf"])
                S.dma("SYNC", out_f[tt * 128:(tt + 1) * 128, :], lnf[:], reads=["lnf"])

            S.op("SYNC", lambda e: e.nop(), waits=S.now("QSP", "QACT"), q="QSP", n=16)
            S.barrier()

        blk = ctx.enter_context(nc.Block())
        S.replay(blk, sems)

    return nc


from concourse.bass_utils import run_bass_kernel_spmd

_CACHE = {}


def _get(name, builder):
    if name not in _CACHE:
        _CACHE[name] = builder()
    return _CACHE[name]


def kernel(**inputs):
    inp = inputs
    ncA = _get("A", build_A)
    mapsA = [hostA(inp, c) for c in range(8)]
    resA = run_bass_kernel_spmd(ncA, mapsA, core_ids=list(range(8))).results
    x1 = [np.asarray(r["x1"]) for r in resA]
    probs = [np.asarray(r["probs"]) for r in resA]
    probs_all = np.concatenate(probs, 0)

    ncB = _get("B", build_B)
    mapsB = []
    for c in range(8):
        m = hostB(inp, c, probs_all)
        m["x1"] = x1[c]
        m["probs_own"] = probs[c]
        mapsB.append(m)
    resB = run_bass_kernel_spmd(ncB, mapsB, core_ids=list(range(8))).results
    x2 = [np.asarray(r["x2"]) for r in resB]
    lat = [np.asarray(r["latent"]) for r in resB]

    ncCD = _get("CD", build_CD)
    mapsCD = []
    for c in range(8):
        b = c // 2
        latT_full = np.concatenate([lat[2 * b], lat[2 * b + 1]], axis=1)
        m = hostCD(inp, c, latT_full)
        m["x2"] = x2[c]
        mapsCD.append(m)
    resCD = run_bass_kernel_spmd(ncCD, mapsCD, core_ids=list(range(8))).results
    out = np.concatenate([np.asarray(r["out"]) for r in resCD], 0)
    return out.reshape(4, 4096, 1024)
